# version 06 (final)
# baseline (speedup 1.0000x reference)
import numpy as np
from contextlib import ExitStack

import concourse.bass as bass
import concourse.mybir as mybir
from concourse.bass_utils import run_bass_kernel_spmd

F32 = mybir.dt.float32
BF16 = mybir.dt.bfloat16
AF = mybir.ActivationFunctionType
ALU = mybir.AluOpType
AX = mybir.AxisListType

D = 1024
H = 4096
KD = D // 128
KH = H // 128
TAPS = 40
NTF = 64
NTA = 1024
SUB = 512
LN_EPS = 1e-5
ABS_EPS = 1e-6

ENGSEM = {"v": "sV", "s": "sS", "t": "sT", "d": "sD", "g": "sG"}


class Prog:
    def __init__(self):
        self.items = []
        self.cnt = {s: 0 for s in ENGSEM.values()}

    def op(self, eng, fn, deps=(), sem=None):
        if sem is None:
            sem = ENGSEM[eng]
        if sem not in self.cnt:
            self.cnt[sem] = 0
        self.cnt[sem] += 16 if eng in ("d", "g") else 1
        tok = (sem, self.cnt[sem])
        self.items.append((eng, fn, [t for t in deps if t is not None], tok, sem))
        return tok


def build(T=4096, stage="full", strict=False, rc=False):
    ntf = T // NTF
    nta = T // NTA
    nsub = NTA // SUB

    nc = bass.Bass("TRN2", target_bir_lowering=False,
                   dynamic_dma_scratch_size=15616,
                   detect_race_conditions=strict or rc)

    xT = nc.dram_tensor("xT", [D, T], F32, kind="ExternalInput").ap()
    wfct = nc.dram_tensor("wfct", [D, H], F32, kind="ExternalInput").ap()
    wprjt = nc.dram_tensor("wprjt", [H, D], F32, kind="ExternalInput").ap()
    w1h = nc.dram_tensor("w1", [D, D], F32, kind="ExternalInput").ap()
    w2h = nc.dram_tensor("w2", [D, D], F32, kind="ExternalInput").ap()
    mwtc = nc.dram_tensor("mwtc", [128, KD * (TAPS + 1)], F32,
                          kind="ExternalInput").ap()
    b1c_h = nc.dram_tensor("b1c", [128, KD], F32, kind="ExternalInput").ap()
    b2c_h = nc.dram_tensor("b2c", [128, KD], F32, kind="ExternalInput").ap()
    lnwc_h = nc.dram_tensor("lnwc", [128, KH], F32, kind="ExternalInput").ap()
    lnbc_h = nc.dram_tensor("lnbc", [128, KH], F32, kind="ExternalInput").ap()
    outT = nc.dram_tensor("outT", [D, T], F32, kind="ExternalOutput").ap()

    st = ExitStack()
    sb = lambda name, shape, dt: st.enter_context(nc.sbuf_tensor(name, shape, dt))

    X = sb("X", [128, KD * T], BF16)
    WQF = sb("WQF", [128, KD * H], BF16)
    LQ = sb("LQ", [128, KH * D], BF16)

    NB_PERM = 2 * SUB + D + KH + 128 + 1
    NB_PH = KH * NTF + KD * NTF + 2 * NTF
    ARB = sb("ARB", [128, NB_PERM + NB_PH], BF16)
    o = 0
    def carve(n):
        nonlocal o
        r = o; o += n
        return r
    murow_o = carve(SUB)
    srow_o = carve(SUB)
    csln_o = carve(D)
    lnb05_o = carve(KH)
    onesrB_o = carve(128)
    onesbB_o = carve(1)
    ph_o = o
    h_o = ph_o
    xn_o = ph_o + KH * NTF
    bcmu_o = xn_o + KD * NTF
    bcs_o = bcmu_o + NTF
    g2_o = bcmu_o
    sq_o = h_o
    sbcA_o = ph_o
    csw1n_o = ph_o + SUB
    sqA_o = ph_o + SUB + D
    absb_o = ph_o
    mscr_o = ph_o + 1024
    assert o <= NB_PERM + NB_PH

    PW = NTA + TAPS
    p_o = 0
    f_o = KD * PW
    wc_o = f_o + KD * NTA
    assert wc_o + KD * D <= KH * D

    ARF = sb("ARF", [128, 1154], F32)
    qin_o = 0
    mwst_o = 0
    tmpF_o = 0
    outf_o = SUB
    thrw_o = 1024
    onesr_o = 1025

    CONST = sb("CONST", [128, 401], F32)
    b1cF_o, b2cF_o = 0, 8
    lnw05_o = 16
    mw0F_o = 48
    thF_o, nthF_o, thP_o, nthP_o = 56, 57, 58, 59
    gpart_o = 60
    gtmp_o = 62
    onesC_o = 63
    ptotF_o = 64
    gvF_o = 72
    epsC_o = 400
    tapsF_o = 80

    PB = [st.enter_context(nc.psum_tensor(f"pb{i}", [128, 512], F32))
          for i in range(8)]

    ALLSEMS = list(ENGSEM.values()) + ["sQ0", "sQ1", "sO0", "sO1"]
    sems = {s: st.enter_context(nc.semaphore(s)) for s in ALLSEMS}

    P = Prog()
    op = P.op
    arb = lambda off, n: ARB[:, off:off + n]
    arf = lambda off, n: ARF[:, off:off + n]
    cst = lambda off, n: CONST[:, off:off + n]
    rowb = lambda off, n: ARB[0:1, off:off + n]
    rowf = lambda off, n: ARF[0:1, off:off + n]
    Xs = lambda cb, t0, n: X[:, cb * T + t0: cb * T + t0 + n]
    Lqs = lambda off, n: LQ[:, off:off + n]

    ones_b = arb(onesbB_o, 1)
    ones_rB = rowb(onesrB_o, 128)

    _xl = None
    for cb in range(KD):
        for c0 in range(0, T, 1024):
            _xl = op("g", lambda e, cb=cb, c0=c0: e.dma_start(
                Xs(cb, c0, min(1024, T - c0)),
                xT[cb * 128:(cb + 1) * 128, c0:c0 + min(1024, T - c0)]),
                deps=[_xl])
    mwld = op("d", lambda e: e.dma_start(arf(mwst_o, KD * 41), mwtc[:, :]))
    b1ld = op("d", lambda e: e.dma_start(cst(b1cF_o, 8), b1c_h[:]))
    b2ld = op("d", lambda e: e.dma_start(cst(b2cF_o, 8), b2c_h[:]))
    lnwld = op("d", lambda e: e.dma_start(cst(lnw05_o, KH), lnwc_h[:]))
    lnbld = op("g", lambda e: e.dma_start(arb(lnb05_o, KH), lnbc_h[:]))
    dburst = lnwld
    burst = lnbld
    xload = [burst] * KD

    mset = []
    mset.append(op("v", lambda e: e.memset(ones_b[:], 1.0)))
    mset.append(op("v", lambda e: e.memset(ones_rB[:], 1.0)))
    mset.append(op("v", lambda e: e.memset(rowf(onesr_o, 128)[:], 1.0)))
    mset.append(op("v", lambda e: e.memset(cst(onesC_o, 1)[:], 1.0)))
    mset.append(op("v", lambda e: e.memset(cst(gpart_o, 2)[:], 0.0)))
    mset.append(op("v", lambda e: e.memset(cst(ptotF_o, 8)[:], 0.0)))
    mset.append(op("v", lambda e: e.memset(cst(epsC_o, 1)[:], LN_EPS)))
    tapsdone = []
    for kb in range(KD):
        op("v", lambda e, kb=kb: e.tensor_scalar(
            cst(tapsF_o + kb * TAPS, TAPS), arf(mwst_o + kb * 41 + 1, TAPS),
            arf(mwst_o + kb * 41, 1), None, op0=ALU.subtract),
            deps=[dburst])
        tapsdone.append(op("v", lambda e, kb=kb: e.tensor_copy(
            cst(mw0F_o + kb, 1), arf(mwst_o + kb * 41, 1))))
    lnscale = op("v", lambda e: e.tensor_scalar(
        cst(lnw05_o, KH), cst(lnw05_o, KH), 0.5, None, op0=ALU.mult),
        deps=[dburst])
    lnb05 = op("v", lambda e: e.tensor_scalar(
        arb(lnb05_o, KH), arb(lnb05_o, KH), 0.5, None, op0=ALU.mult),
        deps=[burst])

    NG = 64

    def gsrc(which, i):
        if which == "fc":
            return wfct[(i // 8) * 128:(i // 8 + 1) * 128,
                        (i % 8) * 512:(i % 8 + 1) * 512]
        return wprjt[(i // 2) * 128:(i // 2 + 1) * 128,
                     (i % 2) * 512:(i % 2 + 1) * 512]

    def gdst(which, i):
        if which == "fc":
            return WQF[:, (i // 8) * H + (i % 8) * 512:
                       (i // 8) * H + (i % 8) * 512 + 512]
        return Lqs((i // 2) * D + (i % 2) * 512, 512)

    def gamma_pass(which, slot, prevdone):
        toks = [None] * NG
        prev_ab = None
        for i in range(NG):
            dep = list(prevdone) if i < 2 else [toks[i - 2]]
            ld = op("d", lambda e, i=i: e.dma_start(
                arf(qin_o + (i % 2) * 512, 512), gsrc(which, i)), deps=dep,
                sem=f"sQ{i % 2}")
            ab = op("s", lambda e, i=i: e.activation(
                arb(absb_o + (i % 2) * 512, 512), arf(qin_o + (i % 2) * 512, 512),
                AF.Abs, accum_out=cst(gtmp_o, 1)),
                deps=[ld, prev_ab] + ([toks[i - 1]] if i else []))
            prev_ab = ab
            toks[i] = op("v", lambda e: e.tensor_tensor(
                cst(gpart_o + slot, 1), cst(gpart_o + slot, 1), cst(gtmp_o, 1),
                op=ALU.add), deps=[ab] + ([toks[i - 1]] if i else []))
        return toks[NG - 1], toks[NG - 2]

    def th_compute(slot, th_o, nth_o, dep):
        t1 = op("t", lambda e, slot=slot: e.matmul(
            PB[7][0:1, 0:1], cst(onesC_o, 1), cst(gpart_o + slot, 1),
            start=True, stop=True), deps=[dep])
        t2 = op("v", lambda e: e.tensor_scalar(
            rowf(thrw_o, 1), PB[7][0:1, 0:1], 0.5 / (D * H), 0.5 * ABS_EPS,
            op0=ALU.mult, op1=ALU.add), deps=[t1])
        t3 = op("t", lambda e: e.matmul(
            PB[7][:, 0:1], rowf(onesr_o, 128), rowf(thrw_o, 1), start=True, stop=True),
            deps=[t2])
        t4 = op("v", lambda e, th_o=th_o: e.tensor_copy(
            cst(th_o, 1), PB[7][:, 0:1]), deps=[t3])
        return op("v", lambda e, th_o=th_o, nth_o=nth_o: e.tensor_scalar(
            cst(nth_o, 1), cst(th_o, 1), -1.0, None, op0=ALU.mult), deps=[t4])

    def ternarize_pass(which, th_o, nth_o, thdep, lqscale=False, extra_dep=None,
                       chunk_done=None):
        wq = [None] * NG
        mmr = [None] * NG
        for i in range(NG):
            dep = [thdep] if i < 2 else [wq[i - 2]]
            if extra_dep is not None and i < 2:
                dep = dep + (list(extra_dep) if isinstance(extra_dep, list)
                             else [extra_dep])
            ld = op("d", lambda e, i=i: e.dma_start(
                arf(qin_o + (i % 2) * 512, 512), gsrc(which, i)), deps=dep,
                sem=f"sQ{i % 2}")
            m = op("v", lambda e, i=i: e.tensor_scalar(
                arb(mscr_o + (i % 2) * 512, 512), arf(qin_o + (i % 2) * 512, 512),
                cst(nth_o, 1), None, op0=ALU.is_le), deps=[ld])
            wdep = [m] if i < 2 else [m, mmr[i - 2]]
            wq[i] = op("v", lambda e, i=i: e.scalar_tensor_tensor(
                gdst(which, i), arf(qin_o + (i % 2) * 512, 512), cst(th_o, 1),
                arb(mscr_o + (i % 2) * 512, 512),
                op0=ALU.is_ge, op1=ALU.subtract), deps=wdep)
            mmr[i] = wq[i]
            if which == "proj" and i % 2 == 1:
                kb = i // 2
                if not lqscale:
                    vp = None
                    for db in range(KD):
                        vp = op("t", lambda e, kb=kb, db=db: e.matmul(
                            PB[4][:, db * KH + kb: db * KH + kb + 1],
                            Lqs(kb * D + db * 128, 128), arb(lnb05_o + kb, 1),
                            start=True, stop=True),
                            deps=[wq[i], wq[i - 1], lnb05] if db == 0 else [])
                    sc = op("v", lambda e, kb=kb: e.tensor_scalar(
                        Lqs(kb * D, D), Lqs(kb * D, D), cst(lnw05_o + kb, 1),
                        None, op0=ALU.mult), deps=[vp, lnscale])
                    c1 = op("t", lambda e, kb=kb: e.matmul(
                        PB[6][0:1, :], ones_b[:], Lqs(kb * D, 512),
                        start=(kb == 0), stop=(kb == KH - 1)), deps=[sc])
                    mmr[i] = op("t", lambda e, kb=kb: e.matmul(
                        PB[7][0:1, :], ones_b[:], Lqs(kb * D + 512, 512),
                        start=(kb == 0), stop=(kb == KH - 1)))
                    wq[i] = sc
                else:
                    sc = op("v", lambda e, kb=kb: e.tensor_scalar(
                        Lqs(kb * D, D), Lqs(kb * D, D), cst(lnw05_o + kb, 1),
                        None, op0=ALU.mult), deps=[wq[i], wq[i - 1]])
                    wq[i] = sc
                    mmr[i] = sc
            if chunk_done is not None:
                chunk_done[i] = wq[i]
        return wq[NG - 1], mmr[NG - 1], wq[NG - 2]

    qlevel = {"q0": 0, "q1a": 0.3, "q1b": 0.6, "q1": 1, "q2": 2}.get(stage, 3)
    quant_done = [tapsdone[-1], lnscale, lnb05, mset[5]]
    if qlevel >= 0.3:
        gfc_a, gfc_b = gamma_pass("fc", 0, [mset[4], tapsdone[-1], lnscale, lnb05])
        quant_done = [gfc_a]
    if qlevel >= 0.6:
        thf = th_compute(0, thF_o, nthF_o, gfc_a)
        quant_done = [thf]
    if qlevel >= 1:
        gpr_a, gpr_b = gamma_pass("proj", 1, [gfc_b, gfc_a])
        thp = th_compute(1, thP_o, nthP_o, gpr_a)
        quant_done = [thf, thp]
    if qlevel >= 2:
        wqf_done, _, wqf_2nd = ternarize_pass("fc", thF_o, nthF_o, thf,
                                              extra_dep=[gpr_a, gpr_b])
        quant_done = [wqf_done]
    if qlevel >= 3:
        lq_done, csmm_done, _ = ternarize_pass("proj", thP_o, nthP_o, thp,
                                               lqscale=False,
                                               extra_dep=[wqf_done, wqf_2nd])
        vproj_ev = None
        for db in range(KD):
            vproj_ev = op("v", lambda e, db=db: e.reduce_sum(
                cst(gvF_o + db, 1), PB[4][:, db * KH:(db + 1) * KH], axis=AX.X),
                deps=[lq_done, csmm_done] if db == 0 else [])
        csl1 = op("v", lambda e: e.tensor_scalar(
            rowb(csln_o, 512), PB[6][0:1, :], -1.0, None, op0=ALU.mult),
            deps=[lq_done, csmm_done])
        csl2 = op("v", lambda e: e.tensor_scalar(
            rowb(csln_o + 512, 512), PB[7][0:1, :], -1.0, None, op0=ALU.mult))
        quant_done = [vproj_ev, csl1, csl2, wqf_done]

    state = {
        "x_ready": list(xload),
        "x_reader": [None] * KD,
        "h_guard": [quant_done[-1]] * 2,
        "pb_free": {i: None for i in range(8)},
        "h_reader": None,
        "xn_reader": None,
        "bc_reader": None,
        "outf_free": [None, None],
    }
    for q in quant_done:
        pass

    def pre_ln(t0, NT, bco_mu, bco_s, sq_base, deps_first, mu_bcast=True):
        sqt = [None, None]
        mm2 = [None, None]
        first_extra = list(deps_first)
        for kb in range(KD):
            dep = [state["x_ready"][kb]]
            if kb == 0:
                dep += first_extra + [state["pb_free"][4], state["pb_free"][6]]
            if kb >= 2:
                dep.append(mm2[kb % 2])
            sqt[kb % 2] = op("v", lambda e, kb=kb, t0=t0, NT=NT, sq_base=sq_base:
                e.tensor_tensor(
                    arb(sq_base + (kb % 2) * NT, NT), Xs(kb, t0, NT),
                    Xs(kb, t0, NT), op=ALU.mult), deps=dep)
            op("t", lambda e, kb=kb, t0=t0, NT=NT: e.matmul(
                PB[4][0:1, 0:NT], ones_b[:], Xs(kb, t0, NT),
                start=(kb == 0), stop=(kb == KD - 1)),
                deps=[sqt[kb % 2]] if kb == 0 else [])
            mm2[kb % 2] = op("t", lambda e, kb=kb, NT=NT, sq_base=sq_base:
                e.matmul(
                    PB[6][0:1, 0:NT], ones_b[:], arb(sq_base + (kb % 2) * NT, NT),
                    start=(kb == 0), stop=(kb == KD - 1)),
                deps=[sqt[kb % 2]])
        stats = mm2[(KD - 1) % 2]
        r1 = op("v", lambda e, NT=NT: e.tensor_scalar(
            rowb(murow_o, NT), PB[4][0:1, 0:NT], 1.0 / D, None, op0=ALU.mult),
            deps=[stats])
        r2 = op("v", lambda e, NT=NT: e.tensor_tensor(
            rowb(srow_o, NT), rowb(murow_o, NT), rowb(murow_o, NT), op=ALU.mult),
            deps=[r1])
        r3 = op("v", lambda e, NT=NT: e.scalar_tensor_tensor(
            rowb(srow_o, NT), PB[6][0:1, 0:NT], 1.0 / D, rowb(srow_o, NT),
            op0=ALU.mult, op1=ALU.subtract), deps=[r2])
        r4 = op("s", lambda e, NT=NT: e.activation(
            rowb(srow_o, NT), rowb(srow_o, NT), AF.Ln, bias=CONST[0:1, epsC_o:epsC_o + 1]),
            deps=[r3, mset[6]])
        r5 = op("s", lambda e, NT=NT: e.activation(
            rowb(srow_o, NT), rowb(srow_o, NT), AF.Exp, scale=-0.5), deps=[r4])
        state["pb_free"][4] = r1
        state["pb_free"][6] = r3
        bt = []
        if mu_bcast:
            bt.append(op("t", lambda e, NT=NT: e.matmul(
                PB[5][:, 0:NT], ones_rB[:], rowb(murow_o, NT),
                start=True, stop=True), deps=[r1, state["pb_free"][5]]))
        b2 = op("t", lambda e, NT=NT, off=(NT if mu_bcast else 0): e.matmul(
            PB[5][:, off:off + NT], ones_rB[:], rowb(srow_o, NT),
            start=True, stop=True), deps=[r5] + ([state["pb_free"][5]] if not mu_bcast else []))
        return r1, (bt[0] if mu_bcast else None), b2

    def ffn(is_final, deps_first, fx=False):
        for ti in range(ntf):
            t0 = ti * NTF
            dfirst = (list(deps_first) if ti == 0 else []) + [state["h_guard"][0]]
            _, bmu, bs = pre_ln(t0, NTF, bcmu_o, bcs_o, sq_o, dfirst)
            e1 = op("s", lambda e: e.activation(
                arb(bcmu_o, NTF), PB[5][:, 0:NTF], AF.Copy, bias=0.0),
                deps=[bmu, state["bc_reader"]])
            e2 = op("s", lambda e: e.activation(
                arb(bcs_o, NTF), PB[5][:, NTF:2 * NTF], AF.Copy, bias=0.0),
                deps=[bs])
            state["pb_free"][5] = e2
            xnt = None
            for kb in range(KD):
                dep = [e1, e2, state["x_ready"][kb]]
                if kb == 0:
                    dep.append(state["xn_reader"])
                xa = op("v", lambda e, kb=kb, t0=t0: e.tensor_tensor(
                    arb(xn_o + kb * NTF, NTF), Xs(kb, t0, NTF),
                    arb(bcmu_o, NTF), op=ALU.subtract), deps=dep)
                xnt = op("v", lambda e, kb=kb: e.tensor_tensor(
                    arb(xn_o + kb * NTF, NTF), arb(xn_o + kb * NTF, NTF),
                    arb(bcs_o, NTF), op=ALU.mult), deps=[xa])
            gel = [None, None]
            g2t = [None, None]
            ist = [None, None]
            fcmm = [None, None]
            for hb in range(KH):
                pb = hb % 2
                dep = [state["pb_free"][pb]]
                if hb < 2:
                    dep.append(xnt)
                    dep.append(state["h_guard"][0])
                for kb in range(KD):
                    fcmm[pb] = op("t", lambda e, hb=hb, kb=kb, pb=pb: e.matmul(
                        PB[pb][:, 0:NTF],
                        WQF[:, kb * H + hb * 128: kb * H + hb * 128 + 128],
                        arb(xn_o + kb * NTF, NTF),
                        start=(kb == 0), stop=(kb == KD - 1)),
                        deps=dep if kb == 0 else [])
                gel[pb] = op("s", lambda e, hb=hb, pb=pb: e.activation(
                    arb(h_o + hb * NTF, NTF), PB[pb][:, 0:NTF], AF.Gelu),
                    deps=[fcmm[pb]])
                state["pb_free"][pb] = gel[pb]
                gdep = [gel[pb]]
                if hb >= 1:
                    gdep.append(ist[(hb - 1) % 2])
                g2t[pb] = op("v", lambda e, hb=hb: e.tensor_tensor(
                    arb(g2_o, NTF), arb(h_o + hb * NTF, NTF),
                    arb(h_o + hb * NTF, NTF), op=ALU.mult), deps=gdep)
                sdep = [gel[pb]]
                if hb == 0:
                    sdep += [state["pb_free"][4]]
                op("t", lambda e, hb=hb: e.matmul(
                    PB[4][0:1, 0:NTF], ones_b[:], arb(h_o + hb * NTF, NTF),
                    start=(hb == 0), stop=(hb == KH - 1)), deps=sdep)
                ist[pb] = op("t", lambda e, hb=hb: e.matmul(
                    PB[6][0:1, 0:NTF], ones_b[:], arb(g2_o, NTF),
                    start=(hb == 0), stop=(hb == KH - 1)),
                    deps=[g2t[pb]] + ([state["pb_free"][6]] if hb == 0 else []))
            istats = ist[(KH - 1) % 2]
            r1 = op("v", lambda e: e.tensor_scalar(
                rowb(murow_o, NTF), PB[4][0:1, 0:NTF], 1.0 / H, None,
                op0=ALU.mult), deps=[istats])
            r2 = op("v", lambda e: e.tensor_tensor(
                rowb(srow_o, NTF), rowb(murow_o, NTF), rowb(murow_o, NTF),
                op=ALU.mult), deps=[r1])
            r3 = op("v", lambda e: e.scalar_tensor_tensor(
                rowb(srow_o, NTF), PB[6][0:1, 0:NTF], 1.0 / H, rowb(srow_o, NTF),
                op0=ALU.mult, op1=ALU.subtract), deps=[r2])
            r4 = op("s", lambda e: e.activation(
                rowb(srow_o, NTF), rowb(srow_o, NTF), AF.Ln,
                bias=CONST[0:1, epsC_o:epsC_o + 1]), deps=[r3])
            r5 = op("s", lambda e: e.activation(
                rowb(srow_o, NTF), rowb(srow_o, NTF), AF.Exp, scale=-0.5),
                deps=[r4])
            state["pb_free"][4] = r1
            state["pb_free"][6] = r3
            bsi = op("t", lambda e: e.matmul(
                PB[5][:, NTF:2 * NTF], ones_rB[:], rowb(srow_o, NTF),
                start=True, stop=True), deps=[r5, e2])
            esi = op("s", lambda e: e.activation(
                arb(bcs_o, NTF), PB[5][:, NTF:2 * NTF], AF.Copy, bias=0.0),
                deps=[bsi])
            state["pb_free"][5] = esi
            state["bc_reader"] = None
            hrd = None
            pv2 = None
            for db in range(KD):
                pb = 2 + db % 2
                dep = [r1, state["pb_free"][pb]]
                op("t", lambda e, db=db, pb=pb: e.matmul(
                    PB[pb][:, 0:NTF], rowb(csln_o + db * 128, 128),
                    rowb(murow_o, NTF), start=True, stop=False), deps=dep)
                pm = None
                for kb in range(KH):
                    pm = op("t", lambda e, db=db, kb=kb, pb=pb: e.matmul(
                        PB[pb][:, 0:NTF], Lqs(kb * D + db * 128, 128),
                        arb(h_o + kb * NTF, NTF),
                        start=False, stop=(kb == KH - 1)),
                        deps=[gel[0], gel[1]] if kb == 0 else [])
                hrd = pm
                v1 = op("v", lambda e, pb=pb: e.tensor_tensor(
                    arf(tmpF_o, NTF), PB[pb][:, 0:NTF], arb(bcs_o, NTF),
                    op=ALU.mult), deps=[pm, esi, pv2])
                if not is_final or fx:
                    v2 = op("v", lambda e, db=db, t0=t0: e.scalar_tensor_tensor(
                        Xs(db, t0, NTF), arf(tmpF_o, NTF), cst(gvF_o + db, 1),
                        Xs(db, t0, NTF), op0=ALU.add, op1=ALU.add),
                        deps=[v1, state["x_reader"][db]])
                    state["x_ready"][db] = v2
                else:
                    slot = db % 2
                    v2 = op("v", lambda e, db=db, slot=slot, t0=t0:
                        e.scalar_tensor_tensor(
                            arf(outf_o + slot * NTF, NTF), arf(tmpF_o, NTF),
                            cst(gvF_o + db, 1), Xs(db, t0, NTF),
                            op0=ALU.add, op1=ALU.add),
                        deps=[v1, state["outf_free"][slot]])
                    dmo = op("d", lambda e, db=db, t0=t0, slot=slot: e.dma_start(
                        outT[db * 128:(db + 1) * 128, t0:t0 + NTF],
                        arf(outf_o + slot * NTF, NTF)),
                        deps=[v2, state["outf_free"][0]])
                    state["outf_free"][0] = dmo
                    state["outf_free"][1] = dmo
                state["pb_free"][pb] = v2
                pv2 = v2
            state["h_guard"] = [hrd, hrd]
            state["xn_reader"] = fcmm[(KH - 1) % 2]
            state["bc_reader"] = v1

    def attn(deps_first, dbg=None):
        w1r = w1h.rearrange("(kb p) m -> p kb m", p=128)
        w2r = w2h.rearrange("(kb p) m -> p kb m", p=128)
        WC = lambda: Lqs(wc_o, KD * D).rearrange("p (kb m) -> p kb m", m=D)
        Pp = lambda kb, c0, n: Lqs(p_o + kb * PW + c0, n)
        Ff = lambda kb, c0, n: Lqs(f_o + kb * NTA + c0, n)
        conv_prev = None
        w2_prev = None
        csw1 = None
        gel_guard = deps_first
        ptot_t = None
        last_v1 = None
        for ta in range(nta):
            ta0 = ta * NTA
            wdep = [w2_prev] + (gel_guard if ta == 0 else [])
            _wl = None
            for kb in range(KD):
                _wl = op("g", lambda e, kb=kb: e.dma_start(
                    WC()[:, kb, :], w1r[:, kb, :]),
                    deps=(wdep if kb == 0 else []) + [_wl])
                w1l1 = _wl
            w1l2 = w1l1
            if ta == 0:
                c1 = c2 = None
                for kb in range(KD):
                    c1 = op("t", lambda e, kb=kb: e.matmul(
                        PB[6][0:1, :], ones_b[:], Lqs(wc_o + kb * D, 512),
                        start=(kb == 0), stop=(kb == KD - 1)),
                        deps=[w1l1, w1l2, state["pb_free"][6]] if kb == 0 else [])
                    c2 = op("t", lambda e, kb=kb: e.matmul(
                        PB[7][0:1, :], ones_b[:], Lqs(wc_o + kb * D + 512, 512),
                        start=(kb == 0), stop=(kb == KD - 1)))
                x1 = op("v", lambda e: e.tensor_scalar(
                    rowb(csw1n_o, 512), PB[6][0:1, :], -1.0, None,
                    op0=ALU.mult), deps=[c1, c2])
                csw1 = op("v", lambda e: e.tensor_scalar(
                    rowb(csw1n_o + 512, 512), PB[7][0:1, :], -1.0, None,
                    op0=ALU.mult))
                state["pb_free"][6] = csw1
                hz = op("v", lambda e: e.memset(
                    Lqs(p_o, KD * PW).rearrange("p (kb c) -> p kb c", c=PW)
                    [:, :, 0:TAPS], 0.0), deps=deps_first)
            else:
                hz = op("v", lambda e: e.tensor_copy(
                    Lqs(p_o, KD * PW).rearrange("p (kb c) -> p kb c", c=PW)
                    [:, :, 0:TAPS],
                    Lqs(p_o, KD * PW).rearrange("p (kb c) -> p kb c", c=PW)
                    [:, :, NTA:NTA + TAPS]), deps=[conv_prev])
            pev = [None] * KD
            for sj in range(nsub):
                t0 = ta0 + sj * SUB
                mu_t, _, bs = pre_ln(t0, SUB, None, None, sqA_o,
                                     [hz] if sj == 0 else [], mu_bcast=False)
                es = op("s", lambda e: e.activation(
                    arb(sbcA_o, SUB), PB[5][:, 0:SUB], AF.Copy, bias=0.0),
                    deps=[bs, last_v1])
                state["pb_free"][5] = es
                for db in range(KD):
                    pb = db % 2
                    dep = [csw1, mu_t, state["pb_free"][pb]]
                    op("t", lambda e, db=db, pb=pb: e.matmul(
                        PB[pb][:, 0:SUB], rowb(csw1n_o + db * 128, 128),
                        rowb(murow_o, SUB), start=True, stop=False), deps=dep)
                    pm = None
                    for kb in range(KD):
                        pm = op("t", lambda e, db=db, kb=kb, pb=pb, t0=t0:
                            e.matmul(
                                PB[pb][:, 0:SUB],
                                Lqs(wc_o + kb * D + db * 128, 128),
                                Xs(kb, t0, SUB),
                                start=False, stop=(kb == KD - 1)),
                            deps=[w1l1, w1l2] if kb == 0 else [])
                        state["x_reader"][kb] = pm
                    v1dep = [pm, es]
                    if db > 0:
                        v1dep.append(pev[db - 1])
                    elif sj > 0:
                        v1dep.append(pev[KD - 1])
                    v1 = op("v", lambda e, pb=pb: e.tensor_tensor(
                        arf(tmpF_o, SUB), PB[pb][:, 0:SUB], arb(sbcA_o, SUB),
                        op=ALU.mult), deps=v1dep)
                    last_v1 = v1
                    pev[db] = op("s", lambda e, db=db, sj=sj: e.activation(
                        Pp(db, TAPS + sj * SUB, SUB), arf(tmpF_o, SUB),
                        AF.Identity, bias=cst(b1cF_o + db, 1)),
                        deps=[v1] + ([conv_prev] if sj == 0 and ta > 0 else []))
                    state["pb_free"][pb] = pev[db]
            if dbg == "dw1":
                for db in range(KD):
                    op("g", lambda e, db=db: e.dma_start(
                        outT[db * 128:(db + 1) * 128, 0:D],
                        Lqs(wc_o + db * D, D)), deps=[w1l1, csw1])
                return [csw1]
            if dbg == "dp":
                for db in range(KD):
                    op("g", lambda e, db=db: e.dma_start(
                        outT[db * 128:(db + 1) * 128, 0:NTA],
                        Pp(db, TAPS, NTA)), deps=[pev[db]])
                return [pev[KD - 1]]
            for db in range(KD):
                rs = op("v", lambda e, db=db: e.reduce_sum(
                    cst(gtmp_o, 1), Pp(db, TAPS, NTA), axis=AX.X),
                    deps=[pev[db], ptot_t])
                ptot_t = op("v", lambda e, db=db: e.tensor_tensor(
                    cst(ptotF_o + db, 1), cst(ptotF_o + db, 1), cst(gtmp_o, 1),
                    op=ALU.add), deps=[rs])
            cvk = [None] * KD
            for s in range(TAPS):
                for kb in range(KD):
                    if s == 0:
                        cvk[kb] = op("v", lambda e, kb=kb: e.tensor_scalar(
                            Ff(kb, 0, NTA), Pp(kb, TAPS, NTA),
                            cst(tapsF_o + kb * TAPS, 1), None, op0=ALU.mult),
                            deps=[pev[KD - 1], w2_prev, hz] if kb == 0 else [])
                    else:
                        cvk[kb] = op("v", lambda e, kb=kb, s=s: e.scalar_tensor_tensor(
                            Ff(kb, 0, NTA), Pp(kb, TAPS - s, NTA),
                            cst(tapsF_o + kb * TAPS + s, 1), Ff(kb, 0, NTA),
                            op0=ALU.mult, op1=ALU.add), deps=[cvk[kb]])
            cv = cvk[KD - 1]
            conv_prev = cv
            if dbg == "df":
                for db in range(KD):
                    op("g", lambda e, db=db: e.dma_start(
                        outT[db * 128:(db + 1) * 128, 0:NTA],
                        Ff(db, 0, NTA)), deps=[cv])
                return [cv]
            _wl2 = None
            for kb in range(KD):
                _wl2 = op("g", lambda e, kb=kb: e.dma_start(
                    WC()[:, kb, :], w2r[:, kb, :]),
                    deps=([pm] if kb == 0 else []) + [_wl2])
                w2l1 = _wl2
            w2l2 = w2l1
            if dbg == "dwb":
                for db in range(KD):
                    op("g", lambda e, db=db: e.dma_start(
                        outT[db * 128:(db + 1) * 128, 0:D],
                        Lqs(wc_o + db * D, D)), deps=[w2l1, conv_prev])
                return [conv_prev]
            for sj in range(nsub):
                t0 = ta0 + sj * SUB
                for db in range(KD):
                    pb = 2 + db % 2
                    wm = None
                    for kb in range(KD):
                        dep = []
                        if kb == 0:
                            dep = [w2l1, w2l2, conv_prev, state["pb_free"][pb]]
                        wm = op("t", lambda e, db=db, kb=kb, pb=pb, sj=sj:
                            e.matmul(
                                PB[pb][:, 0:SUB],
                                Lqs(wc_o + kb * D + db * 128, 128),
                                Ff(kb, sj * SUB, SUB),
                                start=(kb == 0), stop=(kb == KD - 1)), deps=dep)
                    w2_prev = wm
                    if dbg == "dw2" and sj == 0 and db == 0:
                        dv = op("v", lambda e, pb=pb: e.tensor_copy(
                            arf(tmpF_o, SUB), PB[pb][:, 0:SUB]), deps=[wm])
                        op("g", lambda e: e.dma_start(
                            outT[0:128, 0:SUB], arf(tmpF_o, SUB)), deps=[dv])
                        return [dv]
                    vr = op("v", lambda e, db=db, t0=t0, pb=pb: e.tensor_tensor(
                        Xs(db, t0, SUB), Xs(db, t0, SUB), PB[pb][:, 0:SUB],
                        op=ALU.add), deps=[wm, state["x_reader"][db]])
                    state["pb_free"][pb] = vr
                    state["x_ready"][db] = vr
        if dbg in ("ng", "nv"):
            if dbg == "nv":
                dv = op("v", lambda e: e.tensor_copy(
                    arf(tmpF_o, SUB), Xs(0, 0, SUB)),
                    deps=[state["x_ready"][0]])
                op("g", lambda e: e.dma_start(
                    outT[0:128, 0:SUB], arf(tmpF_o, SUB)), deps=[dv])
                return [dv]
            return [state["x_ready"][cb] for cb in range(KD)]
        q0d = None
        if dbg == "dgv":
            q0d = op("v", lambda e: e.tensor_copy(arf(tmpF_o, 8),
                                                  cst(ptotF_o, 8)),
                     deps=[ptot_t])
        q1 = op("v", lambda e: e.tensor_tensor(
            arf(tmpF_o + 8, 8), cst(ptotF_o, 8), cst(mw0F_o, 8), op=ALU.mult),
            deps=[ptot_t, q0d])
        q2 = op("v", lambda e: e.tensor_copy(arb(sq_o, 8), arf(tmpF_o + 8, 8)),
                deps=[q1])
        gm = None
        for db in range(KD):
            for kb in range(KD):
                gm = op("t", lambda e, db=db, kb=kb: e.matmul(
                    PB[7][:, db:db + 1], Lqs(wc_o + kb * D + db * 128, 128),
                    arb(sq_o + kb, 1),
                    start=(kb == 0), stop=(kb == KD - 1)),
                    deps=[q2, w2_prev, state["pb_free"][7]] if kb == 0 else [])
        gv = op("v", lambda e: e.tensor_tensor(
            cst(ptotF_o, 8), PB[7][:, 0:8], cst(b2cF_o, 8), op=ALU.add),
            deps=[gm])
        state["pb_free"][7] = gv
        if dbg == "dgv":
            d1 = op("v", lambda e: e.tensor_copy(arf(tmpF_o + 8, 8),
                                                 cst(ptotF_o, 8)), deps=[gv])
            d2 = op("v", lambda e: e.tensor_copy(arf(tmpF_o + 16, 8),
                                                 arb(sq_o, 8)), deps=[d1])
            op("g", lambda e: e.dma_start(outT[0:128, 0:24], arf(tmpF_o, 24)),
               deps=[d2])
            return [d2]
        xfin = []
        for cb in range(KD):
            xfin.append(op("v", lambda e, cb=cb: e.tensor_scalar(
                Xs(cb, 0, T), Xs(cb, 0, T), cst(ptotF_o + cb, 1), None,
                op0=ALU.add), deps=[state["x_ready"][cb], gv]))
            state["x_ready"][cb] = xfin[cb]
        return xfin

    if stage in ("s1", "s12", "full", "dlq2", "fx", "dp", "df", "dw1", "ng", "nv", "dwb", "dw2", "dgv"):
        ffn(False, quant_done)
    if stage in ("s12", "full", "dlq2", "fx", "dp", "df", "dw1", "ng", "nv", "dwb", "dw2", "dgv"):
        s2fin = attn([state["h_guard"][0], state["xn_reader"],
                      state["bc_reader"]],
                     dbg=stage if stage in ("dp", "df", "dw1", "ng", "nv", "dwb", "dw2", "dgv") else None)
    if stage in ("full", "dlq2", "fx"):
        rq_dep = s2fin[-1]
        lq2, _, _ = ternarize_pass("proj", thP_o, nthP_o, rq_dep, lqscale=True)
        if stage == "dlq2":
            for cb in range(KD):
                op("g", lambda e, cb=cb: e.dma_start(
                    outT[cb * 128:(cb + 1) * 128, 0:min(T, 1024)],
                    Lqs(cb * 4 * D, min(T, 1024))), deps=[lq2])
        else:
            state["h_guard"] = [lq2, lq2]
            state["xn_reader"] = None
            state["bc_reader"] = None
            ffn(True, [lq2], fx=(stage == "fx"))
            if stage == "fx":
                for cb in range(KD):
                    op("g", lambda e, cb=cb: e.dma_start(
                        outT[cb * 128:(cb + 1) * 128, :], Xs(cb, 0, T)),
                        deps=[state["x_ready"][cb]])
    elif stage == "dwq":
        for cb in range(KD):
            op("g", lambda e, cb=cb: e.dma_start(
                outT[cb * 128:(cb + 1) * 128, :], WQF[:, cb * H:cb * H + T]),
                deps=quant_done)
    elif stage == "dlq":
        for cb in range(KD):
            op("g", lambda e, cb=cb: e.dma_start(
                outT[cb * 128:(cb + 1) * 128, :], Lqs(cb * 4 * D, T)),
                deps=quant_done)
    elif stage == "dth":
        dv = op("v", lambda e: e.tensor_copy(arf(tmpF_o, 4), cst(thF_o, 4)),
                deps=quant_done)
        op("g", lambda e: e.dma_start(outT[0:128, 0:4], arf(tmpF_o, 4)),
           deps=[dv])
    elif stage not in ("dp", "df", "dw1", "dwb", "dw2", "nv", "dgv"):
        dl = None
        for cb in range(KD):
            dl = op("g", lambda e, cb=cb: e.dma_start(
                outT[cb * 128:(cb + 1) * 128, :], Xs(cb, 0, T)),
                deps=[state["x_ready"][cb]] + quant_done)

    final_toks = [(s, P.cnt[s]) for s in P.cnt if P.cnt[s] > 0]

    def emit(ename, e):
        mysem = ENGSEM[ename]
        high = {s: 0 for s in P.cnt}
        prev = None
        for eng, fn, deps, tok, osem in P.items:
            if eng != ename:
                continue
            for (s, v) in deps:
                if high[s] < v:
                    e.wait_ge(sems[s], v)
                    high[s] = v
            if strict and prev is not None and high[prev[0]] < prev[1]:
                e.wait_ge(sems[prev[0]], prev[1])
                high[prev[0]] = prev[1]
            inst = fn(e)
            inst.then_inc(sems[osem], 16 if ename in ("d", "g") else 1)
            prev = (osem, tok[1])
        if ename == "d":
            for (s, v) in final_toks:
                if s != "sD" and high[s] < v:
                    e.wait_ge(sems[s], v)
            e.wait_ge(sems["sD"], P.cnt["sD"])

    with nc.Block() as blk:
        @blk.sync
        def _(e):
            emit("d", e)

        @blk.gpsimd
        def _(e):
            emit("g", e)

        @blk.vector
        def _(e):
            emit("v", e)

        @blk.scalar
        def _(e):
            emit("s", e)

        @blk.tensor
        def _(e):
            emit("t", e)

    st.close()
    global _last_prog
    _last_prog = P.items
    return nc


_last_prog = []


def kernel(L0_attn_input_data, L0_attn_W1, L0_attn_bias1, L0_attn_W2,
           L0_attn_bias2, L0_attn_memory_weights, L0_ffn_W_fc, L0_ffn_ln_w,
           L0_ffn_ln_b, L0_ffn_W_proj):
    x = np.asarray(L0_attn_input_data, dtype=np.float32)
    B, T, _ = x.shape
    nc = build(T)
    mw = np.asarray(L0_attn_memory_weights, dtype=np.float32)
    common = {
        "wfct": np.ascontiguousarray(np.asarray(L0_ffn_W_fc, np.float32).T),
        "wprjt": np.ascontiguousarray(np.asarray(L0_ffn_W_proj, np.float32).T),
        "w1": np.ascontiguousarray(np.asarray(L0_attn_W1, np.float32)),
        "w2": np.ascontiguousarray(np.asarray(L0_attn_W2, np.float32)),
        "mwtc": np.ascontiguousarray(
            mw.T.reshape(8, 128, 41).transpose(1, 0, 2).reshape(128, 328)),
        "b1c": np.ascontiguousarray(
            np.asarray(L0_attn_bias1, np.float32).reshape(8, 128).T),
        "b2c": np.ascontiguousarray(
            np.asarray(L0_attn_bias2, np.float32).reshape(8, 128).T),
        "lnwc": np.ascontiguousarray(
            np.asarray(L0_ffn_ln_w, np.float32).reshape(32, 128).T),
        "lnbc": np.ascontiguousarray(
            np.asarray(L0_ffn_ln_b, np.float32).reshape(32, 128).T),
    }
    in_maps = []
    for b in range(B):
        m = dict(common)
        m["xT"] = np.ascontiguousarray(x[b].T)
        in_maps.append(m)
    res = run_bass_kernel_spmd(nc, in_maps, core_ids=list(range(B)))
    out = np.stack([np.asarray(res.results[b]["outT"]).T for b in range(B)])
    return np.ascontiguousarray(out).astype(np.float32)
